# version 13
# baseline (speedup 1.0000x reference)
import numpy as np
import concourse.bass as bass
import concourse.mybir as mybir
from concourse.bass_utils import run_bass_kernel_spmd

F32 = mybir.dt.float32
BF16 = mybir.dt.bfloat16
I32 = mybir.dt.int32
AF = mybir.ActivationFunctionType
ALU = mybir.AluOpType
AXX = mybir.AxisListType.X

B, T, E, H = 4, 4096, 1024, 16
NP_ = H // 2
TOKA = T
TOK = T // 2
NTA = TOKA // 128
NT = TOK // 128
NKT = TOKA // 128
QBS = [768, 768, 512]
EPS = 1e-5
QEPS = 1e-8
THETA = 10000.0
SHUF = list(range(16, 32)) + list(range(0, 16))


def _rho_perm():
    p = []
    for blk in range(2):
        p += [2 * (16 * blk + j) for j in range(16)]
        p += [2 * (16 * blk + j) + 1 for j in range(16)]
    return np.array(p)


def _rope_tables():
    j_of_row = np.empty(64, np.int64)
    sign = np.empty(64, np.float64)
    for r in range(64):
        blk, off = divmod(r % 32, 16)
        j_of_row[r] = 16 * (r // 32) + off
        sign[r] = 1.0 if blk == 1 else -1.0
    freqs = 1.0 / THETA ** (np.arange(0, 64, 2, dtype=np.float32) / 64.0)
    ang = np.arange(TOKA, dtype=np.float32)[None, :] * freqs[:, None]
    C64 = np.cos(ang)[j_of_row, :].astype(np.float32)
    S64 = (np.sin(ang)[j_of_row, :] * sign[:, None]).astype(np.float32)
    return (np.ascontiguousarray(np.concatenate([C64, C64], axis=0)),
            np.ascontiguousarray(np.concatenate([S64, S64], axis=0)))


def _arrange(x, wqkv, bqkv, wffn, bffn):
    perm = _rho_perm()
    swap128 = np.concatenate([np.arange(16, 32), np.arange(0, 16),
                              np.arange(48, 64), np.arange(32, 48)])
    swap128 = np.concatenate([swap128, swap128 + 64])
    wqk = np.empty((NP_, 128, 2048), np.float32)
    bqk = np.empty((NP_, 128, 2), np.float32)
    bqk_sw = np.empty((NP_, 128, 2), np.float32)
    wv = np.empty((NP_, 128, 1024), np.float32)
    bv = np.empty((NP_, 128, 128), np.float32)
    for p in range(NP_):
        rows_q = np.concatenate([64 * (2 * p) + perm, 64 * (2 * p + 1) + perm])
        rows_k = rows_q + E
        pair = np.concatenate([wqkv[rows_q, :], wqkv[rows_k, :]], axis=0)
        wqk[p] = pair.T.reshape(8, 128, 256).transpose(1, 0, 2).reshape(128, 2048)
        bq, bk = bqkv[rows_q], bqkv[rows_k]
        bqk[p, :, 0], bqk[p, :, 1] = bq, bk
        bqk_sw[p, :, 0], bqk_sw[p, :, 1] = bq[swap128], bk[swap128]
        rows_v = np.arange(2 * E + 128 * p, 2 * E + 128 * (p + 1))
        wv[p] = wqkv[rows_v, :].T.reshape(8, 128, 128).transpose(1, 0, 2).reshape(128, 1024)
        bv[p] = np.tile(bqkv[rows_v][None, :], (128, 1))
    C, S = _rope_tables()
    bffn_bc = np.tile(np.asarray(bffn, np.float32)[None, :], (128, 1))
    in_maps = []
    for c in range(8):
        b, hh = divmod(c, 2)
        order = np.concatenate([np.arange(hh * TOK, (hh + 1) * TOK),
                                np.arange((1 - hh) * TOK, (2 - hh) * TOK)])
        in_maps.append({
            "x_all": np.ascontiguousarray(x[b][order]),
            "wqk_arr": wqk, "wv_arr": wv,
            "bqk_arr": bqk, "bqk_sw_arr": bqk_sw, "bv_arr": bv,
            "ropeC": np.ascontiguousarray(C[:, order]),
            "ropeS": np.ascontiguousarray(S[:, order]),
            "wffn": np.ascontiguousarray(wffn),
            "bffn_bc": bffn_bc,
        })
    return in_maps


class Plan:
    def __init__(self):
        self.steps = []
        self.counts = {}

    def emit(self, eng, fn):
        self.steps.append((eng, fn))

    def wait(self, eng, sem, val):
        if val > 0:
            self.steps.append((eng, lambda e, s=sem, v=val: e.wait_ge(s, v)))

    def inc(self, name, n=1):
        self.counts[name] = self.counts.get(name, 0) + n
        return self.counts[name]

    def cur(self, name):
        return self.counts.get(name, 0)

    def run(self, eng_name, eng):
        for nm, fn in self.steps:
            if nm == eng_name:
                fn(eng)


def _obank(qs):
    return (3072 + 66 * qs) if qs < 7 else (3584 + 66 * (qs - 7))


def build_nc():
    nc = bass.Bass("TRN2", target_bir_lowering=False)

    x_all = nc.dram_tensor("x_all", [TOKA, E], F32, kind="ExternalInput").ap()
    wqk_arr = nc.dram_tensor("wqk_arr", [NP_, 128, 2048], F32, kind="ExternalInput").ap()
    wv_arr = nc.dram_tensor("wv_arr", [NP_, 128, 1024], F32, kind="ExternalInput").ap()
    bqk_arr = nc.dram_tensor("bqk_arr", [NP_, 128, 2], F32, kind="ExternalInput").ap()
    bqk_sw_arr = nc.dram_tensor("bqk_sw_arr", [NP_, 128, 2], F32, kind="ExternalInput").ap()
    bv_arr = nc.dram_tensor("bv_arr", [NP_, 128, 128], F32, kind="ExternalInput").ap()
    ropeC_d = nc.dram_tensor("ropeC", [128, TOKA], F32, kind="ExternalInput").ap()
    ropeS_d = nc.dram_tensor("ropeS", [128, TOKA], F32, kind="ExternalInput").ap()
    wffn_d = nc.dram_tensor("wffn", [E, E], F32, kind="ExternalInput").ap()
    bffn_d = nc.dram_tensor("bffn_bc", [128, E], F32, kind="ExternalInput").ap()
    out_d = nc.dram_tensor("out", [TOK, E], F32, kind="ExternalOutput").ap()
    xn_scr = nc.dram_tensor("xn_scr", [TOKA, E], BF16, kind="Internal")
    xq_scr = nc.dram_tensor("xq_scr", [TOK, E], BF16, kind="Internal")
    wq_scr = nc.dram_tensor("wq_scr", [E, E], BF16, kind="Internal")

    import contextlib
    _stk = contextlib.ExitStack()
    with _stk:
        XT = _stk.enter_context(nc.sbuf_tensor("XT", [128, 8, TOKA], BF16))
        ropeC_s = _stk.enter_context(nc.sbuf_tensor("ropeC_s", [128, TOKA], F32))
        ropeS_s = _stk.enter_context(nc.sbuf_tensor("ropeS_s", [128, TOKA], F32))
        QT = _stk.enter_context(nc.sbuf_tensor("QT", [128, TOK], BF16))
        KT = _stk.enter_context(nc.sbuf_tensor("KT", [128, TOKA], BF16))
        V1 = _stk.enter_context(nc.sbuf_tensor("V1", [128, NKT, 130], BF16))
        Pbuf = _stk.enter_context(nc.sbuf_tensor("Pbuf", [128, 2, 1536], BF16))
        O_sb = _stk.enter_context(nc.sbuf_tensor("O_sb", [128, NT, E], BF16))
        xb = _stk.enter_context(nc.sbuf_tensor("xb", [128, 4, E], F32))
        xnb = _stk.enter_context(nc.sbuf_tensor("xnb", [128, 2, E], BF16))
        stsum = _stk.enter_context(nc.sbuf_tensor("stsum", [128, 4], F32))
        stsq = _stk.enter_context(nc.sbuf_tensor("stsq", [128, 4], F32))
        stm = _stk.enter_context(nc.sbuf_tensor("stm", [128, 4], F32))
        stv = _stk.enter_context(nc.sbuf_tensor("stv", [128, 4], F32))
        str0 = _stk.enter_context(nc.sbuf_tensor("str0", [128, 4], F32))
        str1 = _stk.enter_context(nc.sbuf_tensor("str1", [128, 4], F32))
        strstd = _stk.enter_context(nc.sbuf_tensor("strstd", [128, 4], F32))
        wqk_s = _stk.enter_context(nc.sbuf_tensor("wqk_s", [128, 8, 256], BF16))
        wv_s = _stk.enter_context(nc.sbuf_tensor("wv_s", [128, 8, 128], BF16))
        bqk_s = _stk.enter_context(nc.sbuf_tensor("bqk_s", [128, 2], F32))
        bqk_sw_s = _stk.enter_context(nc.sbuf_tensor("bqk_sw_s", [128, 2], F32))
        bv_s = _stk.enter_context(nc.sbuf_tensor("bv_s", [128, 128], F32))
        ru = _stk.enter_context(nc.sbuf_tensor("ru", [128, 512], F32))
        rt1 = _stk.enter_context(nc.sbuf_tensor("rt1", [128, 512], F32))
        rt2 = _stk.enter_context(nc.sbuf_tensor("rt2", [128, 512], F32))
        rsums = _stk.enter_context(nc.sbuf_tensor("rsums", [128, 12], F32))
        rrec = _stk.enter_context(nc.sbuf_tensor("rrec", [128, 12], F32))
        xqi = _stk.enter_context(nc.sbuf_tensor("xqi", [128, E], I32))
        xf3 = _stk.enter_context(nc.sbuf_tensor("xf3", [128, 2, E], F32))
        wqi = _stk.enter_context(nc.sbuf_tensor("wqi", [128, 512], I32))
        lnam = _stk.enter_context(nc.sbuf_tensor("lnam", [128, 2], F32))
        xsc = _stk.enter_context(nc.sbuf_tensor("xsc", [128, NT], F32))
        xrs = _stk.enter_context(nc.sbuf_tensor("xrs", [128, NT], F32))
        wamax = _stk.enter_context(nc.sbuf_tensor("wamax", [128, 8], F32))
        wrs = _stk.enter_context(nc.sbuf_tensor("wrs", [128, 8], F32))
        wsc = _stk.enter_context(nc.sbuf_tensor("wsc", [128, 8], F32))
        bffn_s = _stk.enter_context(nc.sbuf_tensor("bffn_s", [128, E], F32))
        ps = _stk.enter_context(nc.psum_tensor("ps", [128, 4096], F32))
        s_ld = _stk.enter_context(nc.semaphore("s_ld"))
        s_gld = _stk.enter_context(nc.semaphore("s_gld"))
        s_tr = _stk.enter_context(nc.semaphore("s_tr"))
        s_st = _stk.enter_context(nc.semaphore("s_st"))
        s_smm = _stk.enter_context(nc.semaphore("s_smm"))
        s_exp = _stk.enter_context(nc.semaphore("s_exp"))
        s_pv = _stk.enter_context(nc.semaphore("s_pv"))
        s_ve = _stk.enter_context(nc.semaphore("s_ve"))
        s_sc = _stk.enter_context(nc.semaphore("s_sc"))
        s_x = [_stk.enter_context(nc.semaphore(f"s_x{i}")) for i in range(4)]
        s_sn = [_stk.enter_context(nc.semaphore(f"s_sn{i}")) for i in range(2)]
        s_trb = [_stk.enter_context(nc.semaphore(f"s_trb{i}")) for i in range(8)]
        s_sq = [_stk.enter_context(nc.semaphore(f"s_sq{i}")) for i in range(2)]
        s_trq = [_stk.enter_context(nc.semaphore(f"s_trq{i}")) for i in range(2)]
        s_swq = _stk.enter_context(nc.semaphore("s_swq"))
        s_trw = _stk.enter_context(nc.semaphore("s_trw"))
        s_so = [_stk.enter_context(nc.semaphore(f"s_so{i}")) for i in range(2)]
        s_xf = [_stk.enter_context(nc.semaphore(f"s_xf{i}")) for i in range(2)]
        block = _stk.enter_context(nc.Block())
        def junk_ap(i):
            return Pbuf[:, i % 2, 0:1024]

        def xqb_ap(pp):
            return xnb[:, pp, :]

        def wqb_ap(pp, lo, hi):
            return bass.AP(wqk_s, 1024 * pp + lo, [[2048, 128], [1, hi - lo]])

        def x1r_ap(i):
            return ropeC_s[:, 1024 * i:1024 * (i + 1)]

        def ln2b_ap(pp):
            return ropeS_s[:, 1024 * pp:1024 * (pp + 1)]

        def outb_ap(pp, lo=0, hi=E):
            return ropeS_s[:, 2048 + 1024 * pp + lo:2048 + 1024 * pp + hi]

        P = Plan()

        P.emit("sync", lambda e: e.dma_start(out=ropeC_s[:], in_=ropeC_d).then_inc(s_ld, 16))
        P.inc("s_ld", 16)
        P.emit("sync", lambda e: e.dma_start(out=ropeS_s[:], in_=ropeS_d).then_inc(s_ld, 16))
        P.inc("s_ld", 16)
        P.emit("sync", lambda e: e.dma_start(out=bffn_s[:], in_=bffn_d).then_inc(s_ld, 16))
        P.inc("s_ld", 16)
        P.emit("vector", lambda e: e.memset(
            bass.AP(V1, 64, [[NKT * 130, 128], [65, 2], [130, NKT], [1, 1]]), 1.0).then_inc(s_ve, 1))
        P.inc("s_ve", 1)
        P.emit("gpsimd", lambda e: e.dma_start(
            out=wqk_s[:], in_=wqk_arr[0].rearrange("q (a b) -> q a b", a=8)).then_inc(s_gld, 16))
        P.inc("s_gld", 16)
        P.emit("gpsimd", lambda e: e.dma_start(
            out=wv_s[:], in_=wv_arr[0].rearrange("q (a b) -> q a b", a=8)).then_inc(s_gld, 16))
        P.inc("s_gld", 16)
        P.emit("sync", lambda e: e.dma_start(out=bqk_s[:], in_=bqk_arr[0]).then_inc(s_ld, 16))
        P.inc("s_ld", 16)
        P.emit("sync", lambda e: e.dma_start(out=bqk_sw_s[:], in_=bqk_sw_arr[0]).then_inc(s_ld, 16))
        P.inc("s_ld", 16)
        P.emit("sync", lambda e: e.dma_start(out=bv_s[:], in_=bv_arr[0]).then_inc(s_ld, 16))
        P.inc("s_ld", 16)

        norm_ve = {}
        xn_store = {}
        xslot_uses = [0, 0, 0, 0]
        for g in range(NTA // 4):
            ldvals = []
            for i in range(4):
                t = 4 * g + i
                if g > 0:
                    P.wait("sync", s_ve, norm_ve[4 * (g - 1) + i])
                P.emit("sync", lambda e, t=t, i=i: e.dma_start(
                    out=xb[:, i, :],
                    in_=x_all.rearrange("(n p) e -> n p e", p=128)[t]).then_inc(s_x[i], 16))
                xslot_uses[i] += 1
                ldvals.append(16 * xslot_uses[i])
            for i in range(4):
                P.wait("scalar", s_x[i], ldvals[i])
                P.emit("scalar", lambda e, i=i: e.drain())
                P.emit("scalar", lambda e, i=i: e.activation(
                    out=junk_ap(i), in_=xb[:, i, :], func=AF.Identity,
                    accum_out=stsum[:, i:i + 1]))
                P.emit("scalar", lambda e, i=i: e.drain())
                P.emit("scalar", lambda e, i=i: e.activation(
                    out=junk_ap(i), in_=xb[:, i, :], func=AF.Square,
                    accum_out=stsq[:, i:i + 1]).then_inc(s_sc, 1))
                P.inc("s_sc", 1)
            P.wait("vector", s_sc, P.cur("s_sc"))

            def stats(e):
                e.drain()
                e.tensor_scalar(out=stm[:], in0=stsum[:], scalar1=1.0 / E, scalar2=None, op0=ALU.mult)
                e.tensor_scalar(out=stv[:], in0=stsq[:], scalar1=1.0 / E, scalar2=None, op0=ALU.mult)
                e.drain()
                e.tensor_tensor(out=str0[:], in0=stm[:], in1=stm[:], op=ALU.mult)
                e.drain()
                e.tensor_tensor(out=stv[:], in0=stv[:], in1=str0[:], op=ALU.subtract)
                e.drain()
                e.tensor_scalar(out=stv[:], in0=stv[:], scalar1=EPS, scalar2=None,
                                op0=ALU.add).then_inc(s_ve, 1)
            P.emit("vector", stats)
            scv = P.inc("s_ve", 1)
            P.wait("scalar", s_ve, scv)
            P.emit("scalar", lambda e: e.drain())
            P.emit("scalar", lambda e: e.activation(
                out=str0[:], in_=stv[:], func=AF.Sqrt).then_inc(s_sc, 1))
            sqv = P.inc("s_sc", 1)
            P.wait("vector", s_sc, sqv)

            def newton(e):
                e.reciprocal(out=str1[:], in_=str0[:])
                e.drain()
                e.tensor_tensor(out=str0[:], in0=str1[:], in1=str1[:], op=ALU.mult)
                e.drain()
                e.tensor_tensor(out=str0[:], in0=str0[:], in1=stv[:], op=ALU.mult)
                e.drain()
                e.tensor_scalar(out=str0[:], in0=str0[:], scalar1=-0.5, scalar2=1.5,
                                op0=ALU.mult, op1=ALU.add)
                e.drain()
                e.tensor_tensor(out=strstd[:], in0=str1[:], in1=str0[:], op=ALU.mult)
                e.drain()
            P.emit("vector", newton)
            for i in range(4):
                t = 4 * g + i
                if t >= 2:
                    P.wait("vector", s_sn[t % 2], xn_store[t - 2])
                P.emit("vector", lambda e, i=i, t=t: e.tensor_scalar(
                    out=xnb[:, t % 2, :], in0=xb[:, i, :],
                    scalar1=stm[:, i:i + 1], scalar2=strstd[:, i:i + 1],
                    op0=ALU.subtract, op1=ALU.mult).then_inc(s_ve, 1))
                norm_ve[t] = P.inc("s_ve", 1)
                P.wait("gpsimd", s_ve, norm_ve[t])
                P.emit("gpsimd", lambda e, t=t: e.dma_start(
                    out=xn_scr.ap().rearrange("(n p) e -> n p e", p=128)[t],
                    in_=xnb[:, t % 2, :]).then_inc(s_sn[t % 2], 16))
                xn_store[t] = 16 * (t // 2 + 1)
            P.wait("scalar", s_sn[0], xn_store[4 * g + 2])
            P.wait("scalar", s_sn[1], xn_store[4 * g + 3])
            for ec in range(8):
                P.emit("scalar", lambda e, ec=ec, g=g: e.dma_start_transpose(
                    XT[:, ec, 512 * g:512 * (g + 1)],
                    xn_scr.ap()[512 * g:512 * (g + 1), 128 * ec:128 * (ec + 1)]).then_inc(s_trb[g], 16))

        exp_of_set = [0, 0]
        pbuf_freed = [0, 0]
        o_evac_ve = 0

        prev_proj_mm = 0
        prev_evac_ve = 0
        for p in range(NP_):
            if p > 0:
                P.wait("gpsimd", s_smm, prev_proj_mm)
                P.emit("gpsimd", lambda e, p=p: e.dma_start(
                    out=wqk_s[:], in_=wqk_arr[p].rearrange("q (a b) -> q a b", a=8)).then_inc(s_gld, 16))
                P.inc("s_gld", 16)
                P.emit("gpsimd", lambda e, p=p: e.dma_start(
                    out=wv_s[:], in_=wv_arr[p].rearrange("q (a b) -> q a b", a=8)).then_inc(s_gld, 16))
                P.inc("s_gld", 16)
                P.wait("sync", s_ve, prev_evac_ve)
                P.emit("sync", lambda e, p=p: e.dma_start(out=bqk_s[:], in_=bqk_arr[p]).then_inc(s_ld, 16))
                P.inc("s_ld", 16)
                P.emit("sync", lambda e, p=p: e.dma_start(out=bqk_sw_s[:], in_=bqk_sw_arr[p]).then_inc(s_ld, 16))
                P.inc("s_ld", 16)
                P.emit("sync", lambda e, p=p: e.dma_start(out=bv_s[:], in_=bv_arr[p]).then_inc(s_ld, 16))
                P.inc("s_ld", 16)
            gldv = P.cur("s_gld")
            ldv = P.cur("s_ld")

            P.wait("tensor", s_gld, gldv)
            P.wait("tensor", s_ld, ldv)
            if p > 0:
                P.wait("tensor", s_exp, P.cur("s_exp"))
                P.wait("vector", s_pv, P.cur("s_pv"))

            qk_groups = [(tb, fc) for tb in range(8) for fc in (1, 0) if fc == 1 or tb < 4]
            reg_free_ve = {0: 0, 1: 0}
            for gi, (tb, fc) in enumerate(qk_groups):
                reg = 512 * (gi % 2)
                P.wait("tensor", s_ve, reg_free_ve[gi % 2])
                if p == 0:
                    P.wait("tensor", s_trb[tb], 128)

                def qkmm(e, tb=tb, fc=fc, reg=reg):
                    for ec in range(8):
                        ins = e.matmul(ps[:, reg:reg + 512],
                                       wqk_s[:, ec, 128 * fc:128 * (fc + 1)],
                                       XT[:, ec, 512 * tb:512 * (tb + 1)],
                                       start=(ec == 0), stop=(ec == 7))
                    ins.then_inc(s_smm, 1)
                P.emit("tensor", qkmm)
                mmv = P.inc("s_smm", 1)
                P.wait("vector", s_smm, mmv)
                dst = (KT if fc == 1 else QT)

                def qkevac(e, tb=tb, fc=fc, reg=reg, dst=dst):
                    e.drain()
                    e.stream_shuffle(out=ru[:], in_=ps[:, reg:reg + 512], mask=SHUF)
                    e.scalar_tensor_tensor(out=rt1[:], in0=ps[:, reg:reg + 512],
                                           scalar=bqk_s[:, fc:fc + 1],
                                           in1=ropeC_s[:, 512 * tb:512 * (tb + 1)],
                                           op0=ALU.add, op1=ALU.mult)
                    e.drain()
                    e.scalar_tensor_tensor(out=rt2[:], in0=ru[:],
                                           scalar=bqk_sw_s[:, fc:fc + 1],
                                           in1=ropeS_s[:, 512 * tb:512 * (tb + 1)],
                                           op0=ALU.add, op1=ALU.mult)
                    e.drain()
                    e.tensor_tensor(out=dst[:, 512 * tb:512 * (tb + 1)], in0=rt1[:],
                                    in1=rt2[:], op=ALU.add).then_inc(s_ve, 1)
                P.emit("vector", qkevac)
                reg_free_ve[gi % 2] = P.inc("s_ve", 1)

            v_free_ve = {0: 0, 1: 0}
            for vt in range(NKT):
                reg = 1024 if vt % 2 == 0 else 1536
                P.wait("tensor", s_ve, v_free_ve[vt % 2])
                if p == 0:
                    P.wait("tensor", s_trb[vt // 4], 128)

                def vmm(e, vt=vt, reg=reg):
                    for ec in range(8):
                        ins = e.matmul(ps[:, reg:reg + 128],
                                       XT[:, ec, 128 * vt:128 * (vt + 1)],
                                       wv_s[:, ec, :],
                                       start=(ec == 0), stop=(ec == 7))
                    ins.then_inc(s_smm, 1)
                P.emit("tensor", vmm)
                mmv = P.inc("s_smm", 1)
                P.wait("vector", s_smm, mmv)

                def vevac(e, vt=vt, reg=reg):
                    e.drain()
                    e.tensor_tensor(out=V1[:, vt, 0:64], in0=ps[:, reg:reg + 64],
                                    in1=bv_s[:, 0:64], op=ALU.add)
                    e.tensor_tensor(out=V1[:, vt, 65:129], in0=ps[:, reg + 64:reg + 128],
                                    in1=bv_s[:, 64:128], op=ALU.add).then_inc(s_ve, 1)
                P.emit("vector", vevac)
                v_free_ve[vt % 2] = P.inc("s_ve", 1)
            proj_done_ve = P.cur("s_ve")
            prev_proj_mm = P.cur("s_smm")
            prev_evac_ve = proj_done_ve
            P.wait("tensor", s_ve, proj_done_ve)

            for qbi, qbs in enumerate(QBS):
                qb0 = sum(QBS[:qbi])
                nqs = 2 * qbs // 128

                def s_slices(h, qbs=qbs):
                    lo, hi = h * qbs, (h + 1) * qbs
                    cuts = [lo] + [c for c in range(512, 3072, 512) if lo < c < hi] + [hi]
                    return list(zip(cuts[:-1], cuts[1:]))

                pv_first = True
                for kt in range(NKT):
                    pp = kt % 2
                    so = 1536 * pp
                    P.wait("tensor", s_exp, exp_of_set[pp])

                    def smm(e, so=so, kt=kt, qb0=qb0, qbs=qbs):
                        ins = None
                        for h in (0, 1):
                            for (a, b_) in s_slices(h, qbs):
                                ins = e.matmul(
                                    ps[:, so + a:so + b_],
                                    KT[64 * h:64 * (h + 1), 128 * kt:128 * (kt + 1)],
                                    QT[64 * h:64 * (h + 1),
                                       qb0 + a - h * qbs:qb0 + b_ - h * qbs],
                                    start=True, stop=True, tile_position=(64 * h, 0))
                        ins.then_inc(s_smm, 1)
                    P.emit("tensor", smm)
                    mmv = P.inc("s_smm", 1)
                    if kt >= 1:
                        P.wait("tensor", s_exp, P.cur("s_exp"))
                        if pv_first:
                            P.wait("tensor", s_ve, o_evac_ve)
                        prev = 1 - pp
                        ktm1 = kt - 1

                        def pvb(e, prev=prev, qbs=qbs, nqs=nqs, first=pv_first, ktm1=ktm1):
                            for qs in range(nqs):
                                h, ql = divmod(qs, nqs // 2)
                                ob = _obank(qs)
                                ins = e.matmul(
                                    ps[:, ob:ob + 65],
                                    Pbuf[:, prev, h * qbs + 128 * ql:h * qbs + 128 * (ql + 1)],
                                    V1[:, ktm1, 65 * h:65 * h + 65],
                                    start=(first and qs in (0, 7)), stop=False)
                            ins.then_inc(s_pv, 1)
                        P.emit("tensor", pvb)
                        pbuf_freed[prev] = P.inc("s_pv", 1)
                        pv_first = False
                    P.wait("scalar", s_smm, mmv)
                    P.wait("scalar", s_pv, pbuf_freed[pp])

                    def expf(e, so=so, pp=pp, qbs=qbs):
                        e.activation(out=Pbuf[:, pp, 0:2 * qbs], in_=ps[:, so:so + 2 * qbs],
                                     func=AF.Exp, scale=0.125).then_inc(s_exp, 1)
                    P.emit("scalar", expf)
                    exp_of_set[pp] = P.inc("s_exp", 1)
                P.wait("tensor", s_exp, P.cur("s_exp"))
                prev = (NKT - 1) % 2

                def pvt(e, prev=prev, qbs=qbs, nqs=nqs):
                    for qs in range(nqs):
                        h, ql = divmod(qs, nqs // 2)
                        ob = _obank(qs)
                        ins = e.matmul(
                            ps[:, ob:ob + 65],
                            Pbuf[:, prev, h * qbs + 128 * ql:h * qbs + 128 * (ql + 1)],
                            V1[:, NKT - 1, 65 * h:65 * h + 65],
                            start=False, stop=(qs in (6, nqs - 1)))
                    ins.then_inc(s_pv, 1)
                P.emit("tensor", pvt)
                pbuf_freed[prev] = P.inc("s_pv", 1)
                P.wait("vector", s_pv, P.cur("s_pv"))

                def oevac(e, qb0=qb0, nqs=nqs, p=p):
                    e.drain()
                    n0 = min(nqs, 7)
                    e.tensor_copy(out=rsums[:, 0:n0],
                                  in_=bass.AP(ps, 3072 + 64, [[4096, 128], [66, n0], [1, 1]]))
                    if nqs > 7:
                        e.tensor_copy(out=rsums[:, 7:nqs],
                                      in_=bass.AP(ps, 3584 + 64, [[4096, 128], [66, nqs - 7], [1, 1]]))
                    e.drain()
                    e.reciprocal(out=rrec[:, 0:nqs], in_=rsums[:, 0:nqs])
                    e.drain()
                    for qs in range(nqs):
                        h, ql = divmod(qs, nqs // 2)
                        ob = _obank(qs)
                        tt = qb0 // 128 + ql
                        ins = e.tensor_scalar(
                            out=O_sb[:, tt, 128 * p + 64 * h:128 * p + 64 * h + 64],
                            in0=ps[:, ob:ob + 64], scalar1=rrec[:, qs:qs + 1],
                            scalar2=None, op0=ALU.mult)
                    ins.then_inc(s_ve, 1)
                P.emit("vector", oevac)
                o_evac_ve = P.inc("s_ve", 1)

        wf_cons = {2: norm_ve[NTA - 2], 3: norm_ve[NTA - 1]}
        for wt in range(8):
            slot = 2 + wt % 2
            P.wait("sync", s_ve, wf_cons[slot])
            P.emit("sync", lambda e, wt=wt, slot=slot: e.dma_start(
                out=xb[:, slot, :],
                in_=wffn_d.rearrange("(n p) e -> n p e", p=128)[wt]).then_inc(s_x[slot], 16))
            xslot_uses[slot] += 1
            ldv = 16 * xslot_uses[slot]
            P.wait("vector", s_x[slot], ldv)
            P.emit("vector", lambda e, wt=wt, slot=slot: e.reduce_max(
                out=wamax[:, wt:wt + 1], in_=xb[:, slot, :], axis=AXX,
                apply_absolute_value=True).then_inc(s_ve, 1))
            wf_cons[slot] = P.inc("s_ve", 1)

        def wscale(e):
            e.drain()
            e.tensor_scalar(out=wamax[:], in0=wamax[:], scalar1=QEPS, scalar2=None, op0=ALU.max)
            e.drain()
            e.tensor_scalar(out=wsc[:], in0=wamax[:], scalar1=1.0 / 127.0, scalar2=None, op0=ALU.mult)
            e.reciprocal(out=wrs[:], in_=wamax[:])
            e.drain()
            e.tensor_scalar(out=wrs[:], in0=wrs[:], scalar1=127.0, scalar2=None,
                            op0=ALU.mult).then_inc(s_ve, 1)
        P.emit("vector", wscale)
        wsc_ve = P.inc("s_ve", 1)

        wq_store = {}
        wround_done = {2: max(wsc_ve, wf_cons[2]), 3: max(wsc_ve, wf_cons[3])}
        for wt in range(8):
            slot = 2 + wt % 2
            P.wait("sync", s_ve, wround_done[slot])
            P.emit("sync", lambda e, wt=wt, slot=slot: e.dma_start(
                out=xb[:, slot, :],
                in_=wffn_d.rearrange("(n p) e -> n p e", p=128)[wt]).then_inc(s_x[slot], 16))
            xslot_uses[slot] += 1
            ldv = 16 * xslot_uses[slot]
            P.wait("vector", s_x[slot], ldv)
            if wt >= 2:
                P.wait("vector", s_swq, 16 * wt)

            def wround(e, wt=wt, slot=slot):
                ins = None
                for half in range(2):
                    e.drain()
                    e.tensor_scalar(out=ru[:], in0=xb[:, slot, 512 * half:512 * (half + 1)],
                                    scalar1=wrs[:, wt:wt + 1], scalar2=None, op0=ALU.mult)
                    e.drain()
                    e.tensor_copy(out=wqi[:], in_=ru[:])
                    e.drain()
                    e.tensor_copy(out=rt2[:], in_=wqi[:])
                    e.drain()
                    ins = e.tensor_scalar(out=wqb_ap(wt % 2, 512 * half, 512 * (half + 1)),
                                          in0=rt2[:], scalar1=wsc[:, wt:wt + 1],
                                          scalar2=None, op0=ALU.mult)
                ins.then_inc(s_ve, 1)
            P.emit("vector", wround)
            wev = P.inc("s_ve", 1)
            wround_done[slot] = wev
            P.wait("gpsimd", s_ve, wev)
            P.emit("gpsimd", lambda e, wt=wt: e.dma_start(
                out=wq_scr.ap().rearrange("(n p) e -> n p e", p=128)[wt],
                in_=wqb_ap(wt % 2, 0, E)).then_inc(s_swq, 16))
            wq_store[wt] = 16 * (wt + 1)

        P.wait("scalar", s_swq, 128)
        P.wait("scalar", s_ve, o_evac_ve)
        for ec in range(8):
            P.emit("scalar", lambda e, ec=ec: e.dma_start_transpose(
                XT[:, ec, 2048:3072], wq_scr.ap()[:, 128 * ec:128 * (ec + 1)]).then_inc(s_trw, 16))

        x1b_ve = {}
        epi_ve = {}
        xq_tr = {}
        xq_st = {}
        out_st = {}
        xq_tr_tot = [0, 0]
        xq_st_tot = [0, 0]
        out_st_tot = [0, 0]
        ffn_reg_free = {0: 0, 1: 0}
        xf3_uses = [0, 0]

        for g in range(NT // 4):
            ldvals = []
            for i in range(4):
                t = 4 * g + i
                gate = max(x1b_ve.get(t - 4, 0),
                           wround_done[2 + i % 2] if i >= 2 else 0)
                P.wait("sync", s_ve, gate)
                P.emit("sync", lambda e, t=t, i=i: e.dma_start(
                    out=xb[:, i, :],
                    in_=x_all.rearrange("(n p) e -> n p e", p=128)[t]).then_inc(s_x[i], 16))
                xslot_uses[i] += 1
                ldvals.append(16 * xslot_uses[i])
            for i in range(4):
                t = 4 * g + i
                P.wait("vector", s_x[i], ldvals[i])

                def x1build(e, t=t, i=i):
                    e.drain()
                    e.tensor_tensor(out=x1r_ap(i), in0=xb[:, i, :],
                                    in1=O_sb[:, t, :], op=ALU.add).then_inc(s_ve, 1)
                P.emit("vector", x1build)
                x1b_ve[t] = P.inc("s_ve", 1)
                P.wait("scalar", s_ve, x1b_ve[t])
                P.emit("scalar", lambda e: e.drain())
                P.emit("scalar", lambda e, i=i: e.activation(
                    out=junk_ap(i), in_=x1r_ap(i), func=AF.Identity,
                    accum_out=stsum[:, i:i + 1]))
                P.emit("scalar", lambda e: e.drain())
                P.emit("scalar", lambda e, i=i: e.activation(
                    out=junk_ap(i), in_=x1r_ap(i), func=AF.Square,
                    accum_out=stsq[:, i:i + 1]).then_inc(s_sc, 1))
                P.inc("s_sc", 1)
            P.wait("vector", s_sc, P.cur("s_sc"))
            P.emit("vector", stats)
            scv = P.inc("s_ve", 1)
            P.wait("scalar", s_ve, scv)
            P.emit("scalar", lambda e: e.drain())
            P.emit("scalar", lambda e: e.activation(
                out=str0[:], in_=stv[:], func=AF.Sqrt).then_inc(s_sc, 1))
            sqv = P.inc("s_sc", 1)
            P.wait("vector", s_sc, sqv)
            P.emit("vector", newton)
            for i in range(4):
                t = 4 * g + i
                pp = t % 2

                def ln2f(e, i=i, pp=pp):
                    e.tensor_scalar(out=ln2b_ap(pp), in0=x1r_ap(i),
                                    scalar1=stm[:, i:i + 1], scalar2=strstd[:, i:i + 1],
                                    op0=ALU.subtract, op1=ALU.mult)
                    e.drain()
                    e.reduce_max(out=lnam[:, 0:1], in_=ln2b_ap(pp), axis=AXX,
                                 apply_absolute_value=True)
                    e.drain()
                    e.tensor_scalar(out=lnam[:, 0:1], in0=lnam[:, 0:1], scalar1=QEPS,
                                    scalar2=None, op0=ALU.max)
                    e.drain()
                    e.reciprocal(out=lnam[:, 1:2], in_=lnam[:, 0:1])
                    e.drain()
                P.emit("vector", ln2f)

                def scl(e, t=t):
                    e.tensor_scalar(out=xsc[:, t:t + 1], in0=lnam[:, 0:1],
                                    scalar1=1.0 / 127.0, scalar2=None, op0=ALU.mult)
                    e.tensor_scalar(out=xrs[:, t:t + 1], in0=lnam[:, 1:2],
                                    scalar1=127.0, scalar2=None, op0=ALU.mult)
                    e.drain()
                P.emit("vector", scl)
                if t >= 2:
                    P.wait("vector", s_sq[pp], xq_st[t - 2])

                def quant(e, t=t, pp=pp):
                    for half in range(2):
                        e.tensor_scalar(out=ru[:], in0=ln2b_ap(pp)[:, 512 * half:512 * (half + 1)],
                                        scalar1=xrs[:, t:t + 1], scalar2=None, op0=ALU.mult)
                        e.drain()
                        e.tensor_copy(out=xqi[:, 512 * half:512 * (half + 1)], in_=ru[:])
                        e.drain()
                    e.tensor_copy(out=xqb_ap(pp), in_=xqi[:]).then_inc(s_ve, 1)
                P.emit("vector", quant)
                qv = P.inc("s_ve", 1)
                P.wait("gpsimd", s_ve, qv)
                P.emit("gpsimd", lambda e, t=t, pp=pp: e.dma_start(
                    out=xq_scr.ap().rearrange("(n p) e -> n p e", p=128)[t],
                    in_=xqb_ap(pp)).then_inc(s_sq[pp], 16))
                xq_st_tot[pp] += 16
                xq_st[t] = xq_st_tot[pp]
                P.wait("scalar", s_sq[pp], xq_st[t])
                if t >= 2:
                    P.wait("scalar", s_trq[pp], xq_tr[t - 2])
                for ec in range(8):
                    P.emit("scalar", lambda e, ec=ec, t=t, pp=pp: e.dma_start_transpose(
                        XT[:, ec, 128 * t:128 * (t + 1)],
                        xq_scr.ap()[128 * t:128 * (t + 1), 128 * ec:128 * (ec + 1)]).then_inc(s_trq[pp], 16))
                    xq_tr_tot[pp] += 16
                xq_tr[t] = xq_tr_tot[pp]

        for t in range(NT):
            pp = t % 2
            P.wait("sync", s_ve, epi_ve.get(t - 2, 0))
            P.emit("sync", lambda e, t=t, pp=pp: e.dma_start(
                out=xf3[:, pp, :],
                in_=x_all.rearrange("(n p) e -> n p e", p=128)[t]).then_inc(s_xf[pp], 16))
            xf3_uses[pp] += 1
            xfv = 16 * xf3_uses[pp]
            P.wait("tensor", s_trq[pp], xq_tr[t])
            P.wait("tensor", s_trw, 128)
            for dc in range(2):
                P.wait("tensor", s_ve, ffn_reg_free[dc])

                def fmm(e, t=t, dc=dc):
                    for ec in range(8):
                        ins = e.matmul(ps[:, 512 * dc:512 * (dc + 1)],
                                       XT[:, ec, 128 * t:128 * (t + 1)],
                                       XT[:, ec, 2048 + 512 * dc:2048 + 512 * (dc + 1)],
                                       start=(ec == 0), stop=(ec == 7))
                    ins.then_inc(s_smm, 1)
                P.emit("tensor", fmm)
                mmv = P.inc("s_smm", 1)
                P.wait("vector", s_smm, mmv)
                if dc == 0:
                    P.wait("vector", s_xf[pp], xfv)
                    if t >= 2:
                        P.wait("vector", s_so[pp], out_st[t - 2])

                def fevac(e, t=t, dc=dc, pp=pp):
                    e.drain()
                    e.scalar_tensor_tensor(out=rt2[:], in0=ps[:, 512 * dc:512 * (dc + 1)],
                                           scalar=xsc[:, t:t + 1],
                                           in1=bffn_s[:, 512 * dc:512 * (dc + 1)],
                                           op0=ALU.mult, op1=ALU.add)
                    e.drain()
                    e.tensor_tensor(out=rt2[:], in0=rt2[:],
                                    in1=O_sb[:, t, 512 * dc:512 * (dc + 1)], op=ALU.add)
                    e.drain()
                    e.tensor_tensor(out=outb_ap(pp, 512 * dc, 512 * (dc + 1)),
                                    in0=rt2[:], in1=xf3[:, pp, 512 * dc:512 * (dc + 1)],
                                    op=ALU.add).then_inc(s_ve, 1)
                P.emit("vector", fevac)
                ffn_reg_free[dc] = P.inc("s_ve", 1)
            epi_ve[t] = ffn_reg_free[1]
            P.wait("sync", s_ve, epi_ve[t])
            P.emit("sync", lambda e, t=t, pp=pp: e.dma_start(
                out=out_d.rearrange("(n p) e -> n p e", p=128)[t],
                in_=outb_ap(pp)).then_inc(s_so[pp], 16))
            out_st_tot[pp] += 16
            out_st[t] = out_st_tot[pp]
        P.wait("sync", s_so[0], out_st_tot[0])
        P.wait("sync", s_so[1], out_st_tot[1])

        @block.sync
        def _(e):
            P.run("sync", e)

        @block.gpsimd
        def _(e):
            P.run("gpsimd", e)

        @block.tensor
        def _(e):
            P.run("tensor", e)

        @block.vector
        def _(e):
            P.run("vector", e)

        @block.scalar
        def _(e):
            P.run("scalar", e)

    return nc


_NC_CACHE = {}


def kernel(L0_attn_x, L0_attn_W_qkv, L0_attn_b_qkv, L0_ffn_weight, L0_ffn_bias):
    x = np.asarray(L0_attn_x, np.float32)
    wqkv = np.asarray(L0_attn_W_qkv, np.float32)
    bqkv = np.asarray(L0_attn_b_qkv, np.float32)
    wffn = np.asarray(L0_ffn_weight, np.float32)
    bffn = np.asarray(L0_ffn_bias, np.float32)
    in_maps = _arrange(x, wqkv, bqkv, wffn, bffn)
    nc = build_nc()
    res = run_bass_kernel_spmd(nc, in_maps, core_ids=list(range(8)))
    out = np.empty((B, T, E), np.float32)
    for c in range(8):
        b, hh = divmod(c, 2)
        out[b, hh * TOK:(hh + 1) * TOK, :] = res.results[c]["out"]
    return out


# version 14
# speedup vs baseline: 1.0608x; 1.0608x over previous
import numpy as np
import concourse.bass as bass
import concourse.mybir as mybir
from concourse.bass_utils import run_bass_kernel_spmd

F32 = mybir.dt.float32
BF16 = mybir.dt.bfloat16
I32 = mybir.dt.int32
AF = mybir.ActivationFunctionType
ALU = mybir.AluOpType
AXX = mybir.AxisListType.X

B, T, E, H = 4, 4096, 1024, 16
NP_ = H // 2
TOKA = T
TOK = T // 2
NTA = TOKA // 128
NT = TOK // 128
NKT = TOKA // 128
QBS = [768, 768, 512]
EPS = 1e-5
QEPS = 1e-8
THETA = 10000.0
SHUF = list(range(16, 32)) + list(range(0, 16))


def _rho_perm():
    p = []
    for blk in range(2):
        p += [2 * (16 * blk + j) for j in range(16)]
        p += [2 * (16 * blk + j) + 1 for j in range(16)]
    return np.array(p)


def _rope_tables():
    j_of_row = np.empty(64, np.int64)
    sign = np.empty(64, np.float64)
    for r in range(64):
        blk, off = divmod(r % 32, 16)
        j_of_row[r] = 16 * (r // 32) + off
        sign[r] = 1.0 if blk == 1 else -1.0
    freqs = 1.0 / THETA ** (np.arange(0, 64, 2, dtype=np.float32) / 64.0)
    ang = np.arange(TOKA, dtype=np.float32)[None, :] * freqs[:, None]
    C64 = np.cos(ang)[j_of_row, :].astype(np.float32)
    S64 = (np.sin(ang)[j_of_row, :] * sign[:, None]).astype(np.float32)
    return (np.ascontiguousarray(np.concatenate([C64, C64], axis=0)),
            np.ascontiguousarray(np.concatenate([S64, S64], axis=0)))


def _arrange(x, wqkv, bqkv, wffn, bffn):
    perm = _rho_perm()
    swap128 = np.concatenate([np.arange(16, 32), np.arange(0, 16),
                              np.arange(48, 64), np.arange(32, 48)])
    swap128 = np.concatenate([swap128, swap128 + 64])
    wqk = np.empty((NP_, 128, 2048), np.float32)
    bqk = np.empty((NP_, 128, 2), np.float32)
    bqk_sw = np.empty((NP_, 128, 2), np.float32)
    wv = np.empty((NP_, 128, 1024), np.float32)
    bv = np.empty((NP_, 128, 128), np.float32)
    for p in range(NP_):
        rows_q = np.concatenate([64 * (2 * p) + perm, 64 * (2 * p + 1) + perm])
        rows_k = rows_q + E
        pair = np.concatenate([wqkv[rows_q, :], wqkv[rows_k, :]], axis=0)
        wqk[p] = pair.T.reshape(8, 128, 256).transpose(1, 0, 2).reshape(128, 2048)
        bq, bk = bqkv[rows_q], bqkv[rows_k]
        bqk[p, :, 0], bqk[p, :, 1] = bq, bk
        bqk_sw[p, :, 0], bqk_sw[p, :, 1] = bq[swap128], bk[swap128]
        rows_v = np.arange(2 * E + 128 * p, 2 * E + 128 * (p + 1))
        wv[p] = wqkv[rows_v, :].T.reshape(8, 128, 128).transpose(1, 0, 2).reshape(128, 1024)
        bv[p] = np.tile(bqkv[rows_v][None, :], (128, 1))
    C, S = _rope_tables()
    bffn_bc = np.tile(np.asarray(bffn, np.float32)[None, :], (128, 1))
    in_maps = []
    for c in range(8):
        b, hh = divmod(c, 2)
        order = np.concatenate([np.arange(hh * TOK, (hh + 1) * TOK),
                                np.arange((1 - hh) * TOK, (2 - hh) * TOK)])
        in_maps.append({
            "x_all": np.ascontiguousarray(x[b][order]),
            "wqk_arr": wqk, "wv_arr": wv,
            "bqk_arr": bqk, "bqk_sw_arr": bqk_sw, "bv_arr": bv,
            "ropeC": np.ascontiguousarray(C[:, order]),
            "ropeS": np.ascontiguousarray(S[:, order]),
            "wffn": np.ascontiguousarray(wffn),
            "ident": np.eye(128, dtype=np.float32),
            "bffn_bc": bffn_bc,
        })
    return in_maps


class Plan:
    def __init__(self):
        self.steps = []
        self.counts = {}

    def emit(self, eng, fn):
        self.steps.append((eng, fn))

    def wait(self, eng, sem, val):
        if val > 0:
            self.steps.append((eng, lambda e, s=sem, v=val: e.wait_ge(s, v)))

    def inc(self, name, n=1):
        self.counts[name] = self.counts.get(name, 0) + n
        return self.counts[name]

    def cur(self, name):
        return self.counts.get(name, 0)

    def run(self, eng_name, eng):
        for nm, fn in self.steps:
            if nm == eng_name:
                fn(eng)


def _obank(qs):
    return (3072 + 66 * qs) if qs < 7 else (3584 + 66 * (qs - 7))


def build_nc():
    nc = bass.Bass("TRN2", target_bir_lowering=False)

    x_all = nc.dram_tensor("x_all", [TOKA, E], F32, kind="ExternalInput").ap()
    wqk_arr = nc.dram_tensor("wqk_arr", [NP_, 128, 2048], F32, kind="ExternalInput").ap()
    wv_arr = nc.dram_tensor("wv_arr", [NP_, 128, 1024], F32, kind="ExternalInput").ap()
    bqk_arr = nc.dram_tensor("bqk_arr", [NP_, 128, 2], F32, kind="ExternalInput").ap()
    bqk_sw_arr = nc.dram_tensor("bqk_sw_arr", [NP_, 128, 2], F32, kind="ExternalInput").ap()
    bv_arr = nc.dram_tensor("bv_arr", [NP_, 128, 128], F32, kind="ExternalInput").ap()
    ropeC_d = nc.dram_tensor("ropeC", [128, TOKA], F32, kind="ExternalInput").ap()
    ropeS_d = nc.dram_tensor("ropeS", [128, TOKA], F32, kind="ExternalInput").ap()
    wffn_d = nc.dram_tensor("wffn", [E, E], F32, kind="ExternalInput").ap()
    bffn_d = nc.dram_tensor("bffn_bc", [128, E], F32, kind="ExternalInput").ap()
    ident_d = nc.dram_tensor("ident", [128, 128], F32, kind="ExternalInput").ap()
    out_d = nc.dram_tensor("out", [TOK, E], F32, kind="ExternalOutput").ap()

    import contextlib
    _stk = contextlib.ExitStack()
    with _stk:
        XT = _stk.enter_context(nc.sbuf_tensor("XT", [128, 8, TOKA], BF16))
        ropeC_s = _stk.enter_context(nc.sbuf_tensor("ropeC_s", [128, TOKA], F32))
        ropeS_s = _stk.enter_context(nc.sbuf_tensor("ropeS_s", [128, TOKA], F32))
        QT = _stk.enter_context(nc.sbuf_tensor("QT", [128, TOK], BF16))
        KT = _stk.enter_context(nc.sbuf_tensor("KT", [128, TOKA], BF16))
        V1 = _stk.enter_context(nc.sbuf_tensor("V1", [128, NKT, 130], BF16))
        Pbuf = _stk.enter_context(nc.sbuf_tensor("Pbuf", [128, 2, 1536], BF16))
        O_sb = _stk.enter_context(nc.sbuf_tensor("O_sb", [128, NT, E], BF16))
        xb = _stk.enter_context(nc.sbuf_tensor("xb", [128, 4, E], F32))
        xnb = _stk.enter_context(nc.sbuf_tensor("xnb", [128, 2, E], BF16))
        stsum = _stk.enter_context(nc.sbuf_tensor("stsum", [128, 4], F32))
        stsq = _stk.enter_context(nc.sbuf_tensor("stsq", [128, 4], F32))
        stm = _stk.enter_context(nc.sbuf_tensor("stm", [128, 4], F32))
        stv = _stk.enter_context(nc.sbuf_tensor("stv", [128, 4], F32))
        str0 = _stk.enter_context(nc.sbuf_tensor("str0", [128, 4], F32))
        str1 = _stk.enter_context(nc.sbuf_tensor("str1", [128, 4], F32))
        strstd = _stk.enter_context(nc.sbuf_tensor("strstd", [128, 4], F32))
        wqk_s = _stk.enter_context(nc.sbuf_tensor("wqk_s", [128, 8, 256], BF16))
        wv_s = _stk.enter_context(nc.sbuf_tensor("wv_s", [128, 8, 128], BF16))
        bqk_s = _stk.enter_context(nc.sbuf_tensor("bqk_s", [128, 2], F32))
        bqk_sw_s = _stk.enter_context(nc.sbuf_tensor("bqk_sw_s", [128, 2], F32))
        bv_s = _stk.enter_context(nc.sbuf_tensor("bv_s", [128, 128], F32))
        ru = _stk.enter_context(nc.sbuf_tensor("ru", [128, 512], F32))
        rt1 = _stk.enter_context(nc.sbuf_tensor("rt1", [128, 512], F32))
        rt2 = _stk.enter_context(nc.sbuf_tensor("rt2", [128, 512], F32))
        rsums = _stk.enter_context(nc.sbuf_tensor("rsums", [128, 12], F32))
        rrec = _stk.enter_context(nc.sbuf_tensor("rrec", [128, 12], F32))
        xqi = _stk.enter_context(nc.sbuf_tensor("xqi", [128, E], I32))
        xf3 = _stk.enter_context(nc.sbuf_tensor("xf3", [128, 2, E], F32))
        wqi = _stk.enter_context(nc.sbuf_tensor("wqi", [128, 512], I32))
        lnam = _stk.enter_context(nc.sbuf_tensor("lnam", [128, 2], F32))
        xsc = _stk.enter_context(nc.sbuf_tensor("xsc", [128, NT], F32))
        xrs = _stk.enter_context(nc.sbuf_tensor("xrs", [128, NT], F32))
        wamax = _stk.enter_context(nc.sbuf_tensor("wamax", [128, 8], F32))
        wrs = _stk.enter_context(nc.sbuf_tensor("wrs", [128, 8], F32))
        wsc = _stk.enter_context(nc.sbuf_tensor("wsc", [128, 8], F32))
        bffn_s = _stk.enter_context(nc.sbuf_tensor("bffn_s", [128, E], F32))
        ident_s = _stk.enter_context(nc.sbuf_tensor("ident_s", [128, 128], F32))
        ps = _stk.enter_context(nc.psum_tensor("ps", [128, 4096], F32))
        s_ld = _stk.enter_context(nc.semaphore("s_ld"))
        s_gld = _stk.enter_context(nc.semaphore("s_gld"))
        s_tr = _stk.enter_context(nc.semaphore("s_tr"))
        s_st = _stk.enter_context(nc.semaphore("s_st"))
        s_smm = _stk.enter_context(nc.semaphore("s_smm"))
        s_exp = _stk.enter_context(nc.semaphore("s_exp"))
        s_pv = _stk.enter_context(nc.semaphore("s_pv"))
        s_ve = _stk.enter_context(nc.semaphore("s_ve"))
        s_sc = _stk.enter_context(nc.semaphore("s_sc"))
        s_x = [_stk.enter_context(nc.semaphore(f"s_x{i}")) for i in range(4)]
        s_sn = [_stk.enter_context(nc.semaphore(f"s_sn{i}")) for i in range(2)]
        s_trb = [_stk.enter_context(nc.semaphore(f"s_trb{i}")) for i in range(8)]
        s_sq = [_stk.enter_context(nc.semaphore(f"s_sq{i}")) for i in range(2)]
        s_trq = [_stk.enter_context(nc.semaphore(f"s_trq{i}")) for i in range(2)]
        s_swq = _stk.enter_context(nc.semaphore("s_swq"))
        s_trw = _stk.enter_context(nc.semaphore("s_trw"))
        s_so = [_stk.enter_context(nc.semaphore(f"s_so{i}")) for i in range(2)]
        s_xf = [_stk.enter_context(nc.semaphore(f"s_xf{i}")) for i in range(2)]
        block = _stk.enter_context(nc.Block())
        def junk_ap(i):
            return Pbuf[:, i % 2, 0:1024]

        def xqb_ap(pp):
            return xnb[:, pp, :]

        def wqb_ap(pp, lo, hi):
            return bass.AP(wqk_s, 1024 * pp + lo, [[2048, 128], [1, hi - lo]])

        def x1r_ap(i):
            return ropeC_s[:, 1024 * i:1024 * (i + 1)]

        def ln2b_ap(pp):
            return ropeS_s[:, 1024 * pp:1024 * (pp + 1)]

        def outb_ap(pp, lo=0, hi=E):
            return ropeS_s[:, 2048 + 1024 * pp + lo:2048 + 1024 * pp + hi]

        P = Plan()

        P.emit("sync", lambda e: e.dma_start(out=ropeC_s[:], in_=ropeC_d).then_inc(s_ld, 16))
        P.inc("s_ld", 16)
        P.emit("sync", lambda e: e.dma_start(out=ropeS_s[:], in_=ropeS_d).then_inc(s_ld, 16))
        P.inc("s_ld", 16)
        P.emit("sync", lambda e: e.dma_start(out=bffn_s[:], in_=bffn_d).then_inc(s_ld, 16))
        P.inc("s_ld", 16)
        P.emit("sync", lambda e: e.dma_start(out=ident_s[:], in_=ident_d).then_inc(s_ld, 16))
        P.inc("s_ld", 16)
        P.emit("vector", lambda e: e.memset(
            bass.AP(V1, 64, [[NKT * 130, 128], [65, 2], [130, NKT], [1, 1]]), 1.0).then_inc(s_ve, 1))
        P.inc("s_ve", 1)
        P.emit("gpsimd", lambda e: e.dma_start(
            out=wqk_s[:], in_=wqk_arr[0].rearrange("q (a b) -> q a b", a=8)).then_inc(s_gld, 16))
        P.inc("s_gld", 16)
        P.emit("gpsimd", lambda e: e.dma_start(
            out=wv_s[:], in_=wv_arr[0].rearrange("q (a b) -> q a b", a=8)).then_inc(s_gld, 16))
        P.inc("s_gld", 16)
        P.emit("sync", lambda e: e.dma_start(out=bqk_s[:], in_=bqk_arr[0]).then_inc(s_ld, 16))
        P.inc("s_ld", 16)
        P.emit("sync", lambda e: e.dma_start(out=bqk_sw_s[:], in_=bqk_sw_arr[0]).then_inc(s_ld, 16))
        P.inc("s_ld", 16)
        P.emit("sync", lambda e: e.dma_start(out=bv_s[:], in_=bv_arr[0]).then_inc(s_ld, 16))
        P.inc("s_ld", 16)
        ident_ld = P.cur("s_ld")

        norm_ve = {}
        xslot_uses = [0, 0, 0, 0]
        treg_free = [0, 0]
        trn_mm = {}
        ln_evac = {}

        def xt_ap(t, ec0):
            return bass.AP(XT, ec0 * TOKA + 128 * t, [[8 * TOKA, 128], [TOKA, 4], [1, 128]])

        for g in range(NTA // 4):
            ldvals = []
            for i in range(4):
                t = 4 * g + i
                if g > 0:
                    P.wait("sync", s_ve, norm_ve[4 * (g - 1) + i])
                P.emit("sync", lambda e, t=t, i=i: e.dma_start(
                    out=xb[:, i, :],
                    in_=x_all.rearrange("(n p) e -> n p e", p=128)[t]).then_inc(s_x[i], 16))
                xslot_uses[i] += 1
                ldvals.append(16 * xslot_uses[i])
            for i in range(4):
                P.wait("scalar", s_x[i], ldvals[i])
                P.emit("scalar", lambda e, i=i: e.drain())
                P.emit("scalar", lambda e, i=i: e.activation(
                    out=junk_ap(i), in_=xb[:, i, :], func=AF.Identity,
                    accum_out=stsum[:, i:i + 1]))
                P.emit("scalar", lambda e, i=i: e.drain())
                P.emit("scalar", lambda e, i=i: e.activation(
                    out=junk_ap(i), in_=xb[:, i, :], func=AF.Square,
                    accum_out=stsq[:, i:i + 1]).then_inc(s_sc, 1))
                P.inc("s_sc", 1)
            P.wait("vector", s_sc, P.cur("s_sc"))

            def stats(e):
                e.drain()
                e.tensor_scalar(out=stm[:], in0=stsum[:], scalar1=1.0 / E, scalar2=None, op0=ALU.mult)
                e.tensor_scalar(out=stv[:], in0=stsq[:], scalar1=1.0 / E, scalar2=None, op0=ALU.mult)
                e.drain()
                e.tensor_tensor(out=str0[:], in0=stm[:], in1=stm[:], op=ALU.mult)
                e.drain()
                e.tensor_tensor(out=stv[:], in0=stv[:], in1=str0[:], op=ALU.subtract)
                e.drain()
                e.tensor_scalar(out=stv[:], in0=stv[:], scalar1=EPS, scalar2=None,
                                op0=ALU.add).then_inc(s_ve, 1)
            P.emit("vector", stats)
            scv = P.inc("s_ve", 1)
            P.wait("scalar", s_ve, scv)
            P.emit("scalar", lambda e: e.drain())
            P.emit("scalar", lambda e: e.activation(
                out=str0[:], in_=stv[:], func=AF.Sqrt).then_inc(s_sc, 1))
            sqv = P.inc("s_sc", 1)
            P.wait("vector", s_sc, sqv)

            def newton(e):
                e.reciprocal(out=str1[:], in_=str0[:])
                e.drain()
                e.tensor_tensor(out=str0[:], in0=str1[:], in1=str1[:], op=ALU.mult)
                e.drain()
                e.tensor_tensor(out=str0[:], in0=str0[:], in1=stv[:], op=ALU.mult)
                e.drain()
                e.tensor_scalar(out=str0[:], in0=str0[:], scalar1=-0.5, scalar2=1.5,
                                op0=ALU.mult, op1=ALU.add)
                e.drain()
                e.tensor_tensor(out=strstd[:], in0=str1[:], in1=str0[:], op=ALU.mult)
                e.drain()
            P.emit("vector", newton)
            for i in range(4):
                t = 4 * g + i
                pp = t % 2
                if t >= 2:
                    P.wait("vector", s_smm, trn_mm[t - 2])
                P.emit("vector", lambda e, i=i, pp=pp: e.tensor_scalar(
                    out=xf3[:, pp, :], in0=xb[:, i, :],
                    scalar1=stm[:, i:i + 1], scalar2=strstd[:, i:i + 1],
                    op0=ALU.subtract, op1=ALU.mult).then_inc(s_ve, 1))
                norm_ve[t] = P.inc("s_ve", 1)
                for h in range(2):
                    if t == 0 and h == 0:
                        P.wait("tensor", s_ld, ident_ld)
                    P.wait("tensor", s_ve, max(norm_ve[t], treg_free[h]))

                    def trn(e, t=t, pp=pp, h=h):
                        for j in range(4):
                            ins = e.transpose(ps[:, 512 * h + 128 * j:512 * h + 128 * (j + 1)],
                                              xf3[:, pp, 512 * h + 128 * j:512 * h + 128 * (j + 1)],
                                              ident_s[:])
                        ins.then_inc(s_smm, 1)
                    P.emit("tensor", trn)
                    mmv = P.inc("s_smm", 1)
                    P.wait("vector", s_smm, mmv)

                    def tevac(e, t=t, h=h):
                        e.drain()
                        e.tensor_copy(out=xt_ap(t, 4 * h),
                                      in_=ps[:, 512 * h:512 * (h + 1)]).then_inc(s_ve, 1)
                    P.emit("vector", tevac)
                    treg_free[h] = P.inc("s_ve", 1)
                trn_mm[t] = P.cur("s_smm")
                ln_evac[t] = P.cur("s_ve")
        ln_block_ready = [ln_evac[4 * g + 3] for g in range(NTA // 4)]

        exp_of_set = [0, 0]
        pbuf_freed = [0, 0]
        o_evac_ve = 0

        prev_proj_mm = 0
        prev_evac_ve = 0
        for p in range(NP_):
            if p > 0:
                P.wait("gpsimd", s_smm, prev_proj_mm)
                P.emit("gpsimd", lambda e, p=p: e.dma_start(
                    out=wqk_s[:], in_=wqk_arr[p].rearrange("q (a b) -> q a b", a=8)).then_inc(s_gld, 16))
                P.inc("s_gld", 16)
                P.emit("gpsimd", lambda e, p=p: e.dma_start(
                    out=wv_s[:], in_=wv_arr[p].rearrange("q (a b) -> q a b", a=8)).then_inc(s_gld, 16))
                P.inc("s_gld", 16)
                P.wait("sync", s_ve, prev_evac_ve)
                P.emit("sync", lambda e, p=p: e.dma_start(out=bqk_s[:], in_=bqk_arr[p]).then_inc(s_ld, 16))
                P.inc("s_ld", 16)
                P.emit("sync", lambda e, p=p: e.dma_start(out=bqk_sw_s[:], in_=bqk_sw_arr[p]).then_inc(s_ld, 16))
                P.inc("s_ld", 16)
                P.emit("sync", lambda e, p=p: e.dma_start(out=bv_s[:], in_=bv_arr[p]).then_inc(s_ld, 16))
                P.inc("s_ld", 16)
            gldv = P.cur("s_gld")
            ldv = P.cur("s_ld")

            P.wait("tensor", s_gld, gldv)
            P.wait("tensor", s_ld, ldv)
            if p > 0:
                P.wait("tensor", s_exp, P.cur("s_exp"))
                P.wait("vector", s_pv, P.cur("s_pv"))

            qk_groups = [(tb, fc) for tb in range(8) for fc in (1, 0) if fc == 1 or tb < 4]
            reg_free_ve = {0: 0, 1: 0}
            for gi, (tb, fc) in enumerate(qk_groups):
                reg = 1024 + 512 * (gi % 2)
                P.wait("tensor", s_ve, reg_free_ve[gi % 2])
                if p == 0:
                    P.wait("tensor", s_ve, ln_block_ready[tb])

                def qkmm(e, tb=tb, fc=fc, reg=reg):
                    for ec in range(8):
                        ins = e.matmul(ps[:, reg:reg + 512],
                                       wqk_s[:, ec, 128 * fc:128 * (fc + 1)],
                                       XT[:, ec, 512 * tb:512 * (tb + 1)],
                                       start=(ec == 0), stop=(ec == 7))
                    ins.then_inc(s_smm, 1)
                P.emit("tensor", qkmm)
                mmv = P.inc("s_smm", 1)
                P.wait("vector", s_smm, mmv)
                dst = (KT if fc == 1 else QT)

                def qkevac(e, tb=tb, fc=fc, reg=reg, dst=dst):
                    e.drain()
                    e.stream_shuffle(out=ru[:], in_=ps[:, reg:reg + 512], mask=SHUF)
                    e.scalar_tensor_tensor(out=rt1[:], in0=ps[:, reg:reg + 512],
                                           scalar=bqk_s[:, fc:fc + 1],
                                           in1=ropeC_s[:, 512 * tb:512 * (tb + 1)],
                                           op0=ALU.add, op1=ALU.mult)
                    e.drain()
                    e.scalar_tensor_tensor(out=rt2[:], in0=ru[:],
                                           scalar=bqk_sw_s[:, fc:fc + 1],
                                           in1=ropeS_s[:, 512 * tb:512 * (tb + 1)],
                                           op0=ALU.add, op1=ALU.mult)
                    e.drain()
                    e.tensor_tensor(out=dst[:, 512 * tb:512 * (tb + 1)], in0=rt1[:],
                                    in1=rt2[:], op=ALU.add).then_inc(s_ve, 1)
                P.emit("vector", qkevac)
                reg_free_ve[gi % 2] = P.inc("s_ve", 1)

            v_free_ve = {0: 0, 1: 0}
            for vt in range(NKT):
                reg = 2048 if vt % 2 == 0 else 2560
                P.wait("tensor", s_ve, v_free_ve[vt % 2])
                if p == 0:
                    P.wait("tensor", s_ve, ln_block_ready[vt // 4])

                def vmm(e, vt=vt, reg=reg):
                    for ec in range(8):
                        ins = e.matmul(ps[:, reg:reg + 128],
                                       XT[:, ec, 128 * vt:128 * (vt + 1)],
                                       wv_s[:, ec, :],
                                       start=(ec == 0), stop=(ec == 7))
                    ins.then_inc(s_smm, 1)
                P.emit("tensor", vmm)
                mmv = P.inc("s_smm", 1)
                P.wait("vector", s_smm, mmv)

                def vevac(e, vt=vt, reg=reg):
                    e.drain()
                    e.tensor_tensor(out=V1[:, vt, 0:64], in0=ps[:, reg:reg + 64],
                                    in1=bv_s[:, 0:64], op=ALU.add)
                    e.tensor_tensor(out=V1[:, vt, 65:129], in0=ps[:, reg + 64:reg + 128],
                                    in1=bv_s[:, 64:128], op=ALU.add).then_inc(s_ve, 1)
                P.emit("vector", vevac)
                v_free_ve[vt % 2] = P.inc("s_ve", 1)
            proj_done_ve = P.cur("s_ve")
            prev_proj_mm = P.cur("s_smm")
            prev_evac_ve = proj_done_ve
            P.wait("tensor", s_ve, proj_done_ve)

            for qbi, qbs in enumerate(QBS):
                qb0 = sum(QBS[:qbi])
                nqs = 2 * qbs // 128

                def s_slices(h, qbs=qbs):
                    lo, hi = h * qbs, (h + 1) * qbs
                    cuts = [lo] + [c for c in range(512, 3072, 512) if lo < c < hi] + [hi]
                    return list(zip(cuts[:-1], cuts[1:]))

                pv_first = True
                for kt in range(NKT):
                    pp = kt % 2
                    so = 1536 * pp
                    P.wait("tensor", s_exp, exp_of_set[pp])

                    def smm(e, so=so, kt=kt, qb0=qb0, qbs=qbs):
                        ins = None
                        for h in (0, 1):
                            for (a, b_) in s_slices(h, qbs):
                                ins = e.matmul(
                                    ps[:, so + a:so + b_],
                                    KT[64 * h:64 * (h + 1), 128 * kt:128 * (kt + 1)],
                                    QT[64 * h:64 * (h + 1),
                                       qb0 + a - h * qbs:qb0 + b_ - h * qbs],
                                    start=True, stop=True, tile_position=(64 * h, 0))
                        ins.then_inc(s_smm, 1)
                    P.emit("tensor", smm)
                    mmv = P.inc("s_smm", 1)
                    if kt >= 1:
                        P.wait("tensor", s_exp, P.cur("s_exp"))
                        if pv_first:
                            P.wait("tensor", s_ve, o_evac_ve)
                        prev = 1 - pp
                        ktm1 = kt - 1

                        def pvb(e, prev=prev, qbs=qbs, nqs=nqs, first=pv_first, ktm1=ktm1):
                            for qs in range(nqs):
                                h, ql = divmod(qs, nqs // 2)
                                ob = _obank(qs)
                                ins = e.matmul(
                                    ps[:, ob:ob + 65],
                                    Pbuf[:, prev, h * qbs + 128 * ql:h * qbs + 128 * (ql + 1)],
                                    V1[:, ktm1, 65 * h:65 * h + 65],
                                    start=(first and qs in (0, 7)), stop=False)
                            ins.then_inc(s_pv, 1)
                        P.emit("tensor", pvb)
                        pbuf_freed[prev] = P.inc("s_pv", 1)
                        pv_first = False
                    P.wait("scalar", s_smm, mmv)
                    P.wait("scalar", s_pv, pbuf_freed[pp])

                    def expf(e, so=so, pp=pp, qbs=qbs):
                        e.activation(out=Pbuf[:, pp, 0:2 * qbs], in_=ps[:, so:so + 2 * qbs],
                                     func=AF.Exp, scale=0.125).then_inc(s_exp, 1)
                    P.emit("scalar", expf)
                    exp_of_set[pp] = P.inc("s_exp", 1)
                P.wait("tensor", s_exp, P.cur("s_exp"))
                prev = (NKT - 1) % 2

                def pvt(e, prev=prev, qbs=qbs, nqs=nqs):
                    for qs in range(nqs):
                        h, ql = divmod(qs, nqs // 2)
                        ob = _obank(qs)
                        ins = e.matmul(
                            ps[:, ob:ob + 65],
                            Pbuf[:, prev, h * qbs + 128 * ql:h * qbs + 128 * (ql + 1)],
                            V1[:, NKT - 1, 65 * h:65 * h + 65],
                            start=False, stop=(qs in (6, nqs - 1)))
                    ins.then_inc(s_pv, 1)
                P.emit("tensor", pvt)
                pbuf_freed[prev] = P.inc("s_pv", 1)
                P.wait("vector", s_pv, P.cur("s_pv"))

                def oevac(e, qb0=qb0, nqs=nqs, p=p):
                    e.drain()
                    n0 = min(nqs, 7)
                    e.tensor_copy(out=rsums[:, 0:n0],
                                  in_=bass.AP(ps, 3072 + 64, [[4096, 128], [66, n0], [1, 1]]))
                    if nqs > 7:
                        e.tensor_copy(out=rsums[:, 7:nqs],
                                      in_=bass.AP(ps, 3584 + 64, [[4096, 128], [66, nqs - 7], [1, 1]]))
                    e.drain()
                    e.reciprocal(out=rrec[:, 0:nqs], in_=rsums[:, 0:nqs])
                    e.drain()
                    for qs in range(nqs):
                        h, ql = divmod(qs, nqs // 2)
                        ob = _obank(qs)
                        tt = qb0 // 128 + ql
                        ins = e.tensor_scalar(
                            out=O_sb[:, tt, 128 * p + 64 * h:128 * p + 64 * h + 64],
                            in0=ps[:, ob:ob + 64], scalar1=rrec[:, qs:qs + 1],
                            scalar2=None, op0=ALU.mult)
                    ins.then_inc(s_ve, 1)
                P.emit("vector", oevac)
                o_evac_ve = P.inc("s_ve", 1)

        wam_ve = []
        wf_cons = {2: norm_ve[NTA - 2], 3: norm_ve[NTA - 1]}
        for wt in range(8):
            slot = 2 + wt % 2
            P.wait("sync", s_ve, wf_cons[slot])
            P.emit("sync", lambda e, wt=wt, slot=slot: e.dma_start(
                out=xb[:, slot, :],
                in_=wffn_d.rearrange("(n p) e -> n p e", p=128)[wt]).then_inc(s_x[slot], 16))
            xslot_uses[slot] += 1
            ldv = 16 * xslot_uses[slot]
            P.wait("vector", s_x[slot], ldv)
            P.emit("vector", lambda e, wt=wt, slot=slot: e.reduce_max(
                out=wamax[:, wt:wt + 1], in_=xb[:, slot, :], axis=AXX,
                apply_absolute_value=True).then_inc(s_ve, 1))
            wf_cons[slot] = P.inc("s_ve", 1)
            wam_ve.append(wf_cons[slot])

        def wscale(e):
            e.drain()
            e.tensor_scalar(out=wamax[:], in0=wamax[:], scalar1=QEPS, scalar2=None, op0=ALU.max)
            e.drain()
            e.tensor_scalar(out=wsc[:], in0=wamax[:], scalar1=1.0 / 127.0, scalar2=None, op0=ALU.mult)
            e.reciprocal(out=wrs[:], in_=wamax[:])
            e.drain()
            e.tensor_scalar(out=wrs[:], in0=wrs[:], scalar1=127.0, scalar2=None,
                            op0=ALU.mult).then_inc(s_ve, 1)
        P.emit("vector", wscale)
        wsc_ve = P.inc("s_ve", 1)

        attn_final_exp = P.cur("s_exp")
        wround_done = {2: max(wsc_ve, wf_cons[2]), 3: max(wsc_ve, wf_cons[3])}
        wq_trn_mm = 0
        for wt in range(8):
            slot = 2 + wt % 2
            P.wait("sync", s_ve, wround_done[slot])
            P.emit("sync", lambda e, wt=wt, slot=slot: e.dma_start(
                out=xb[:, slot, :],
                in_=wffn_d.rearrange("(n p) e -> n p e", p=128)[wt]).then_inc(s_x[slot], 16))
            xslot_uses[slot] += 1
            ldv = 16 * xslot_uses[slot]
            P.wait("vector", s_x[slot], ldv)
            last_wr = 0
            for h in range(2):
                if wq_trn_mm > 0:
                    P.wait("vector", s_smm, wq_trn_mm)

                def wround(e, wt=wt, slot=slot, h=h):
                    e.drain()
                    e.tensor_scalar(out=ru[:], in0=xb[:, slot, 512 * h:512 * (h + 1)],
                                    scalar1=wrs[:, wt:wt + 1], scalar2=None, op0=ALU.mult)
                    e.drain()
                    e.tensor_copy(out=wqi[:], in_=ru[:])
                    e.drain()
                    e.tensor_copy(out=rt2[:], in_=wqi[:])
                    e.drain()
                    e.tensor_scalar(out=ru[:], in0=rt2[:], scalar1=wsc[:, wt:wt + 1],
                                    scalar2=None, op0=ALU.mult).then_inc(s_ve, 1)
                P.emit("vector", wround)
                wr_ve = P.inc("s_ve", 1)
                P.wait("tensor", s_ve, max(wr_ve, treg_free[h]))
                if wt == 0 and h == 0:
                    P.wait("tensor", s_exp, attn_final_exp)
                    P.wait("tensor", s_pv, P.cur("s_pv"))

                def wtrn(e, h=h):
                    for j in range(4):
                        ins = e.transpose(ps[:, 512 * h + 128 * j:512 * h + 128 * (j + 1)],
                                          ru[:, 128 * j:128 * (j + 1)], ident_s[:])
                    ins.then_inc(s_smm, 1)
                P.emit("tensor", wtrn)
                wq_trn_mm = P.inc("s_smm", 1)
                P.wait("vector", s_smm, wq_trn_mm)

                def wtevac(e, wt=wt, h=h):
                    e.drain()
                    e.tensor_copy(
                        out=bass.AP(XT, (4 * h) * TOKA + 2048 + 128 * wt,
                                    [[8 * TOKA, 128], [TOKA, 4], [1, 128]]),
                        in_=ps[:, 512 * h:512 * (h + 1)]).then_inc(s_ve, 1)
                P.emit("vector", wtevac)
                treg_free[h] = P.inc("s_ve", 1)
            wround_done[slot] = P.cur("s_ve")
        wq_ready_ve = P.cur("s_ve")

        x1b_ve = {}
        epi_ve = {}
        xtq_ready = {}
        f1_trn_mm = {}
        out_st = {}
        out_st_tot = [0, 0]
        ffn_reg_free = {0: 0, 1: 0}
        xf3_uses = [0, 0]
        prev_q_trn = wq_trn_mm

        for g in range(NT // 4):
            ldvals = []
            for i in range(4):
                t = 4 * g + i
                gate = max(x1b_ve.get(t - 4, 0),
                           wround_done[2 + i % 2] if i >= 2 else 0)
                P.wait("sync", s_ve, gate)
                P.emit("sync", lambda e, t=t, i=i: e.dma_start(
                    out=xb[:, i, :],
                    in_=x_all.rearrange("(n p) e -> n p e", p=128)[t]).then_inc(s_x[i], 16))
                xslot_uses[i] += 1
                ldvals.append(16 * xslot_uses[i])
            for i in range(4):
                t = 4 * g + i
                P.wait("vector", s_x[i], ldvals[i])

                def x1build(e, t=t, i=i):
                    e.drain()
                    e.tensor_tensor(out=x1r_ap(i), in0=xb[:, i, :],
                                    in1=O_sb[:, t, :], op=ALU.add).then_inc(s_ve, 1)
                P.emit("vector", x1build)
                x1b_ve[t] = P.inc("s_ve", 1)
                P.wait("scalar", s_ve, x1b_ve[t])
                P.emit("scalar", lambda e: e.drain())
                P.emit("scalar", lambda e, i=i: e.activation(
                    out=junk_ap(i), in_=x1r_ap(i), func=AF.Identity,
                    accum_out=stsum[:, i:i + 1]))
                P.emit("scalar", lambda e: e.drain())
                P.emit("scalar", lambda e, i=i: e.activation(
                    out=junk_ap(i), in_=x1r_ap(i), func=AF.Square,
                    accum_out=stsq[:, i:i + 1]).then_inc(s_sc, 1))
                P.inc("s_sc", 1)
            P.wait("vector", s_sc, P.cur("s_sc"))
            P.emit("vector", stats)
            scv = P.inc("s_ve", 1)
            P.wait("scalar", s_ve, scv)
            P.emit("scalar", lambda e: e.drain())
            P.emit("scalar", lambda e: e.activation(
                out=str0[:], in_=stv[:], func=AF.Sqrt).then_inc(s_sc, 1))
            sqv = P.inc("s_sc", 1)
            P.wait("vector", s_sc, sqv)
            P.emit("vector", newton)
            for i in range(4):
                t = 4 * g + i
                pp = t % 2

                def ln2f(e, i=i, pp=pp):
                    e.tensor_scalar(out=ln2b_ap(pp), in0=x1r_ap(i),
                                    scalar1=stm[:, i:i + 1], scalar2=strstd[:, i:i + 1],
                                    op0=ALU.subtract, op1=ALU.mult)
                    e.drain()
                    e.reduce_max(out=lnam[:, 0:1], in_=ln2b_ap(pp), axis=AXX,
                                 apply_absolute_value=True)
                    e.drain()
                    e.tensor_scalar(out=lnam[:, 0:1], in0=lnam[:, 0:1], scalar1=QEPS,
                                    scalar2=None, op0=ALU.max)
                    e.drain()
                    e.reciprocal(out=lnam[:, 1:2], in_=lnam[:, 0:1])
                    e.drain()
                P.emit("vector", ln2f)

                def scl(e, t=t):
                    e.tensor_scalar(out=xsc[:, t:t + 1], in0=lnam[:, 0:1],
                                    scalar1=1.0 / 127.0, scalar2=None, op0=ALU.mult)
                    e.tensor_scalar(out=xrs[:, t:t + 1], in0=lnam[:, 1:2],
                                    scalar1=127.0, scalar2=None, op0=ALU.mult)
                    e.drain()
                P.emit("vector", scl)
                for h in range(2):
                    if prev_q_trn > 0:
                        P.wait("vector", s_smm, prev_q_trn)

                    def quant(e, t=t, pp=pp, h=h):
                        e.tensor_scalar(out=ru[:], in0=ln2b_ap(pp)[:, 512 * h:512 * (h + 1)],
                                        scalar1=xrs[:, t:t + 1], scalar2=None, op0=ALU.mult)
                        e.drain()
                        e.tensor_copy(out=xqi[:, 0:512], in_=ru[:])
                        e.drain()
                        e.tensor_copy(out=ru[:], in_=xqi[:, 0:512]).then_inc(s_ve, 1)
                    P.emit("vector", quant)
                    q_ve = P.inc("s_ve", 1)
                    P.wait("tensor", s_ve, max(q_ve, treg_free[h]))

                    def qtrn(e, h=h):
                        for j in range(4):
                            ins = e.transpose(ps[:, 512 * h + 128 * j:512 * h + 128 * (j + 1)],
                                              ru[:, 128 * j:128 * (j + 1)], ident_s[:])
                        ins.then_inc(s_smm, 1)
                    P.emit("tensor", qtrn)
                    prev_q_trn = P.inc("s_smm", 1)
                    P.wait("vector", s_smm, prev_q_trn)

                    def qtevac(e, t=t, h=h):
                        e.drain()
                        e.tensor_copy(out=xt_ap(t, 4 * h),
                                      in_=ps[:, 512 * h:512 * (h + 1)]).then_inc(s_ve, 1)
                    P.emit("vector", qtevac)
                    treg_free[h] = P.inc("s_ve", 1)
                f1_trn_mm[t] = prev_q_trn
                xtq_ready[t] = P.cur("s_ve")

        for t in range(NT):
            pp = t % 2
            P.wait("sync", s_ve, epi_ve.get(t - 2, 0))
            P.emit("sync", lambda e, t=t, pp=pp: e.dma_start(
                out=xf3[:, pp, :],
                in_=x_all.rearrange("(n p) e -> n p e", p=128)[t]).then_inc(s_xf[pp], 16))
            xf3_uses[pp] += 1
            xfv = 16 * xf3_uses[pp]
            P.wait("tensor", s_ve, max(xtq_ready[t], wq_ready_ve))
            for dc in range(2):
                reg = 1024 + 512 * dc
                P.wait("tensor", s_ve, ffn_reg_free[dc])

                def fmm(e, t=t, dc=dc, reg=reg):
                    for ec in range(8):
                        ins = e.matmul(ps[:, reg:reg + 512],
                                       XT[:, ec, 128 * t:128 * (t + 1)],
                                       XT[:, ec, 2048 + 512 * dc:2048 + 512 * (dc + 1)],
                                       start=(ec == 0), stop=(ec == 7))
                    ins.then_inc(s_smm, 1)
                P.emit("tensor", fmm)
                mmv = P.inc("s_smm", 1)
                P.wait("vector", s_smm, mmv)
                if dc == 0:
                    P.wait("vector", s_xf[pp], xfv)
                    if t >= 2:
                        P.wait("vector", s_so[pp], out_st[t - 2])

                def fevac(e, t=t, dc=dc, pp=pp, reg=reg):
                    e.drain()
                    e.scalar_tensor_tensor(out=rt2[:], in0=ps[:, reg:reg + 512],
                                           scalar=xsc[:, t:t + 1],
                                           in1=bffn_s[:, 512 * dc:512 * (dc + 1)],
                                           op0=ALU.mult, op1=ALU.add)
                    e.drain()
                    e.tensor_tensor(out=rt2[:], in0=rt2[:],
                                    in1=O_sb[:, t, 512 * dc:512 * (dc + 1)], op=ALU.add)
                    e.drain()
                    e.tensor_tensor(out=outb_ap(pp, 512 * dc, 512 * (dc + 1)),
                                    in0=rt2[:], in1=xf3[:, pp, 512 * dc:512 * (dc + 1)],
                                    op=ALU.add).then_inc(s_ve, 1)
                P.emit("vector", fevac)
                ffn_reg_free[dc] = P.inc("s_ve", 1)
            epi_ve[t] = ffn_reg_free[1]
            P.wait("sync", s_ve, epi_ve[t])
            P.emit("sync", lambda e, t=t, pp=pp: e.dma_start(
                out=out_d.rearrange("(n p) e -> n p e", p=128)[t],
                in_=outb_ap(pp)).then_inc(s_so[pp], 16))
            out_st_tot[pp] += 16
            out_st[t] = out_st_tot[pp]
        P.wait("sync", s_so[0], out_st_tot[0])
        P.wait("sync", s_so[1], out_st_tot[1])

        @block.sync
        def _(e):
            P.run("sync", e)

        @block.gpsimd
        def _(e):
            P.run("gpsimd", e)

        @block.tensor
        def _(e):
            P.run("tensor", e)

        @block.vector
        def _(e):
            P.run("vector", e)

        @block.scalar
        def _(e):
            P.run("scalar", e)

    return nc


_NC_CACHE = {}


def kernel(L0_attn_x, L0_attn_W_qkv, L0_attn_b_qkv, L0_ffn_weight, L0_ffn_bias):
    x = np.asarray(L0_attn_x, np.float32)
    wqkv = np.asarray(L0_attn_W_qkv, np.float32)
    bqkv = np.asarray(L0_attn_b_qkv, np.float32)
    wffn = np.asarray(L0_ffn_weight, np.float32)
    bffn = np.asarray(L0_ffn_bias, np.float32)
    in_maps = _arrange(x, wqkv, bqkv, wffn, bffn)
    nc = build_nc()
    res = run_bass_kernel_spmd(nc, in_maps, core_ids=list(range(8)))
    out = np.empty((B, T, E), np.float32)
    for c in range(8):
        b, hh = divmod(c, 2)
        out[b, hh * TOK:(hh + 1) * TOK, :] = res.results[c]["out"]
    return out
